# version 28
# speedup vs baseline: 1.2414x; 1.0020x over previous
import contextlib
import numpy as np
import ml_dtypes
import concourse.bass as bass
import concourse.mybir as mybir
from concourse.bass_utils import run_bass_kernel_spmd

F32 = mybir.dt.float32
BF16 = mybir.dt.bfloat16
I8 = mybir.dt.int8
AX = mybir.AxisListType
ALU = mybir.AluOpType
AF = mybir.ActivationFunctionType

EPS = 1e-5
NEG = -1e30
QSCALE = float(1.0 / np.sqrt(128.0))


class Cfg:
    def __init__(self, D=2048, H=16, I=8192, NBQ=16, WNB=16):
        self.D, self.H, self.I, self.NBQ, self.WNB = D, H, I, NBQ, WNB
        self.NBKV = NBQ + WNB
        self.DC = D // 128
        self.IC = I // 128
        self.TOK = NBQ * 128
        self.KV = self.NBKV * 128
        self.NQC = self.TOK // 512
        assert D == H * 128 and self.TOK % 512 == 0 and I % 2048 == 0


CFG = Cfg()
ALL_SEMS = ("qS", "qA0", "qA1", "qA2", "qA3", "qG2", "qW2", "qW3", "qX0", "qX1", "qT0", "qT1", "qW0", "qW1",
            "qH0", "qH1", "qC0", "qC1", "qX30", "qX31", "qM",
            "qT20", "qT21", "qG0", "qG1", "qD0", "qD1", "qT30", "qT31",
            "ePE", "eACT", "eDVE", "ePOOL")


class KB:
    def __init__(self, nc, block):
        self.nc, self.block = nc, block
        self.c = {}
        self.sems = {}
        for name in ALL_SEMS:
            self.sems[name] = nc.alloc_semaphore(name)
            self.c[name] = 0

    def inc(self, instr, name, n=1):
        instr.then_inc(self.sems[name], n)
        self.c[name] += n
        return self.c[name]

    def wait(self, eng, name, val=None):
        v = self.c[name] if val is None else val
        if v > 0:
            eng.wait_ge(self.sems[name], v)

    def join(self, eng, names=ALL_SEMS):
        for n in names:
            self.wait(eng, n)

    def barrier(self):
        tot = dict(self.c)
        for attr in ("sync", "tensor", "scalar", "vector", "gpsimd"):
            def body(eng, tot=tot):
                for name, v in tot.items():
                    if v > 0:
                        eng.wait_ge(self.sems[name], v)
            getattr(self.block, attr)(body)


def _ln_chain(kb, block, src, dst, st, sqscr, eps_ap, pre_waits=None,
              extra_norm_waits=None):
    D = src.shape[-1]

    def a_sq(a):
        for nm, val in (pre_waits or []):
            kb.wait(a, nm, val)
        kb.inc(a.activation(sqscr, src, AF.Square, bias=0.0, scale=1.0,
                            accum_out=st[:, 1:2]), "eACT")
    block.scalar(a_sq)
    SQ_ACT = kb.c["eACT"]

    def v_st(v):
        for nm, val in (pre_waits or []):
            kb.wait(v, nm, val)
        kb.inc(v.tensor_reduce(st[:, 0:1], src, AX.X, ALU.add), "eDVE")
        kb.wait(v, "eDVE")
        kb.wait(v, "eACT", SQ_ACT)
        v.tensor_scalar_mul(st[:, 2:3], st[:, 0:1], 1.0 / D)
        kb.inc(v.tensor_scalar_mul(st[:, 3:4], st[:, 1:2], 1.0 / D), "eDVE")
        kb.wait(v, "eDVE")
        kb.inc(v.scalar_tensor_tensor(
            out=st[:, 4:5], in0=st[:, 2:3], scalar=st[:, 2:3], in1=st[:, 3:4],
            op0=ALU.mult, op1=ALU.subtract), "eDVE")
        kb.wait(v, "eDVE")
        kb.inc(v.tensor_scalar_mul(st[:, 5:6], st[:, 4:5], -1.0), "eDVE")
    block.vector(v_st)

    def a_sd(a):
        kb.wait(a, "eDVE")
        kb.inc(a.activation(st[:, 6:7], st[:, 5:6], AF.Sqrt, bias=eps_ap, scale=1.0),
               "eACT")
    block.scalar(a_sd)

    def v_r(v):
        kb.wait(v, "eACT")
        kb.inc(v.reciprocal(st[:, 7:8], st[:, 6:7]), "eDVE")
        kb.wait(v, "eDVE")
        kb.inc(v.scalar_tensor_tensor(
            out=st[:, 0:1], in0=st[:, 2:3], scalar=-1.0, in1=st[:, 7:8],
            op0=ALU.mult, op1=ALU.mult), "eDVE")
    block.vector(v_r)
    ST_DVE = kb.c["eDVE"]

    def v_n(v):
        kb.wait(v, "eDVE", ST_DVE)
        if extra_norm_waits:
            for nm, val in extra_norm_waits:
                kb.wait(v, nm, val)
        kb.inc(v.tensor_scalar(dst, src, st[:, 2:3], st[:, 7:8],
                               ALU.subtract, ALU.mult), "eDVE")
    block.vector(v_n)
    return kb.c["eDVE"]


def build_program(cfg=CFG, dev_stop=None):
    D, H, I = cfg.D, cfg.H, cfg.I
    DC, IC = cfg.DC, cfg.IC
    NBQ, NBKV, WNB = cfg.NBQ, cfg.NBKV, cfg.WNB
    TOK, KV, NQC = cfg.TOK, cfg.KV, cfg.NQC

    nc = bass.Bass("TRN2", target_bir_lowering=False)

    xkv = nc.dram_tensor("xkv", [KV, D], F32, kind="ExternalInput").ap()
    wq = nc.dram_tensor("wq", [D, D], F32, kind="ExternalInput").ap()
    wk = nc.dram_tensor("wk", [D, D], F32, kind="ExternalInput").ap()
    wv = nc.dram_tensor("wv", [D, D], F32, kind="ExternalInput").ap()
    wo = nc.dram_tensor("wo", [D, D], F32, kind="ExternalInput").ap()
    wg_t = nc.dram_tensor("wg_t", [128, DC * I], I8, kind="ExternalInput").ap()
    wu_t = nc.dram_tensor("wu_t", [128, DC * I], I8, kind="ExternalInput").ap()
    wd_t = nc.dram_tensor("wd_t", [128, IC * D], I8, kind="ExternalInput").ap()
    sg_in = nc.dram_tensor("sg", [128, IC], F32, kind="ExternalInput").ap()
    su_in = nc.dram_tensor("su", [128, IC], F32, kind="ExternalInput").ap()
    sd_in = nc.dram_tensor("sd", [128, DC], F32, kind="ExternalInput").ap()
    kvb_in = nc.dram_tensor("kvb", [128, NBKV], F32, kind="ExternalInput").ap()

    y = nc.dram_tensor("y", [TOK, D], F32, kind="ExternalOutput").ap()

    sk = "ExternalOutput" if dev_stop == "p1" else "Internal"
    kt_s = nc.dram_tensor("kt_s", [D, KV], BF16, kind=sk).ap()
    q_s = nc.dram_tensor("q_s", [D, TOK], BF16, kind=sk).ap()
    v_s = nc.dram_tensor("v_s", [KV, D], BF16, kind=sk).ap()
    sk2 = "ExternalOutput" if dev_stop == "p2" else "Internal"
    ctx_s = nc.dram_tensor("ctx_s", [D, TOK], BF16, kind=sk2).ap()
    sk3 = "ExternalOutput" if dev_stop == "p3a" else "Internal"
    xmid_s = nc.dram_tensor("xmid_s", [TOK, D], BF16, kind=sk3).ap()

    pw = [nc.alloc_psum_tensor(f"pw{i}", [128, 1024], F32).ap() for i in range(2)]
    pn = [nc.alloc_psum_tensor(f"pn{i}", [128, 512], F32).ap() for i in range(4)]
    pb = [pw[0][:, 0:512], pw[0][:, 512:1024], pw[1][:, 0:512], pw[1][:, 512:1024],
          pn[0], pn[1], pn[2], pn[3]]

    ones_col = nc.alloc_sbuf_tensor("ones_col", [128, 1], BF16).ap()
    ones_row = nc.alloc_sbuf_tensor("ones_row", [1, 128], BF16).ap()
    m01_diag = nc.alloc_sbuf_tensor("m01_diag", [128, 128], BF16).ap()
    m01_old = nc.alloc_sbuf_tensor("m01_old", [128, 128], BF16).ap()
    id128 = nc.alloc_sbuf_tensor("id128", [128, 128], BF16).ap()
    ones128 = nc.alloc_sbuf_tensor("ones128", [128, 128], BF16).ap()
    eps_sb = nc.alloc_sbuf_tensor("eps_sb", [128, 1], F32).ap()
    zeros512 = nc.alloc_sbuf_tensor("zeros512", [128, 512], BF16).ap()
    kvb_sb = nc.alloc_sbuf_tensor("kvb_sb", [128, NBKV], F32).ap()
    sg_sb = nc.alloc_sbuf_tensor("sg_sb", [128, IC], F32).ap()
    su_sb = nc.alloc_sbuf_tensor("su_sb", [128, IC], F32).ap()
    sd_sb = nc.alloc_sbuf_tensor("sd_sb", [128, DC], F32).ap()

    with nc.Block() as block:
        kb = KB(nc, block)
        c = kb.c

        def s_pro(sync):
            kb.inc(sync.dma_start(out=kvb_sb[:], in_=kvb_in[:, :]), "qS", 16)
            kb.inc(sync.dma_start(out=sg_sb[:], in_=sg_in[:, :]), "qS", 16)
            kb.inc(sync.dma_start(out=su_sb[:], in_=su_in[:, :]), "qS", 16)
            kb.inc(sync.dma_start(out=sd_sb[:], in_=sd_in[:, :]), "qS", 16)
        block.sync(s_pro)

        def v_pro(v):
            v.memset(ones_col[:], 1.0)
            v.memset(ones_row[:], 1.0)
            v.memset(ones128[:], 1.0)
            v.memset(eps_sb[:], EPS)
            kb.inc(v.memset(zeros512[:], 0.0), "eDVE")
        block.vector(v_pro)

        def g_pro(g):
            kb.wait(g, "eDVE")
            kb.inc(g.affine_select(m01_diag[:], ones128[:], [[1, 128]], ALU.is_ge,
                                   0.0, base=0, channel_multiplier=-1), "ePOOL")
            kb.wait(g, "ePOOL")
            kb.inc(g.tensor_tensor(m01_old[:], ones128[:], m01_diag[:],
                                   ALU.subtract), "ePOOL")
            kb.inc(g.affine_select(id128[:], ones128[:], [[1, 128]], ALU.is_ge,
                                   0.0, base=-1, channel_multiplier=-1), "ePOOL")
            kb.wait(g, "ePOOL")
            kb.inc(g.tensor_tensor(id128[:], m01_diag[:], id128[:],
                                   ALU.subtract), "ePOOL")
        block.gpsimd(g_pro)

        with contextlib.ExitStack() as _stk:
            xlnT_g = _stk.enter_context(nc.sbuf_tensor("xlnT", [128, DC * KV], BF16))
            xt_g = _stk.enter_context(nc.sbuf_tensor("p1_xt", [128, 2 * D], BF16))
            xln_g = _stk.enter_context(nc.sbuf_tensor("p1_xln", [128, 2 * D], BF16))
            sq_g = _stk.enter_context(nc.sbuf_tensor("p1_sq", [128, 2 * D], BF16))
            st_g = _stk.enter_context(nc.sbuf_tensor("p1_st", [128, 16], F32))
            wb_g = _stk.enter_context(nc.sbuf_tensor("p1_wb", [128, 2 * D], BF16))
            ev_g = _stk.enter_context(nc.sbuf_tensor("p1_ev", [128, 4 * 512], BF16))
            wvb_g = _stk.enter_context(nc.sbuf_tensor("p1_wvb", [128, 2 * DC * max(512, D // 4)], BF16))
            xlnT = xlnT_g.ap()
            xt = [xt_g.ap()[:, i * D:(i + 1) * D] for i in range(2)]
            xln = [xln_g.ap()[:, i * D:(i + 1) * D] for i in range(2)]
            sqs = [sq_g.ap()[:, i * D:(i + 1) * D] for i in range(2)]
            st = [st_g.ap()[:, i * 8:(i + 1) * 8] for i in range(2)]
            wb = [wb_g.ap()[:, i * D:(i + 1) * D] for i in range(2)]
            wvb = wvb_g.ap()
            ev = [ev_g.ap()[:, i * 512:(i + 1) * 512] for i in range(4)]

            ln_act = {}
            tr_cnt = {}
            tr_pe = {}
            ld_cnt = {}

            def emit_block(t):
                p = t % 2

                def s_ld(g, t=t, p=p):
                    if t >= 2:
                        kb.wait(g, "eDVE", ln_act[t - 2])
                    else:
                        kb.wait(g, "ePOOL")
                    ld_cnt[t] = kb.inc(
                        g.dma_start(out=xt[p][:],
                                    in_=xkv[t * 128:(t + 1) * 128, :]),
                        "qX%d" % (t % 2), 16)
                block.gpsimd(s_ld)

                pre = [("qX%d" % (t % 2), ld_cnt[t])]
                if t >= 2:
                    pre.append(("eDVE", ln_act[t - 2]))
                extra = [("ePE", tr_pe[t - 2])] if t >= 2 else None
                ln_act[t] = _ln_chain(kb, block, xt[p][:], xln[p][:], st[p], sqs[p][:],
                                      eps_sb, pre_waits=pre, extra_norm_waits=extra)

                def t_tr(t_, t=t, p=p):
                    kb.wait(t_, "eDVE", ln_act[t])
                    if t >= 2:
                        kb.wait(t_, "eDVE", tr_cnt[t - 2])
                    for dc in range(DC):
                        bank, k = divmod(dc, 4)
                        mm = t_.matmul(pn[bank][:, k * 128:(k + 1) * 128],
                                       xln[p][:, dc * 128:(dc + 1) * 128],
                                       id128[:], start=(k == 0), stop=(k == 3))
                        if k == 3 or dc == DC - 1:
                            kb.inc(mm, "ePE")
                    tr_pe[t] = c["ePE"]
                block.tensor(t_tr)

                def v_tr(v, t=t):
                    kb.wait(v, "ePE", tr_pe[t])
                    xr = xlnT.rearrange("p (dc kv) -> p dc kv", kv=KV)
                    for bank in range((DC + 3) // 4):
                        nb = min(4, DC - bank * 4)
                        dst = xr[:, bank * 4:bank * 4 + nb, t * 128:(t + 1) * 128]
                        srcw = pn[bank][:, 0:nb * 128].rearrange(
                            "p (k cc) -> p k cc", cc=128)
                        kb.inc(v.tensor_copy(dst, srcw), "eDVE")
                    tr_cnt[t] = c["eDVE"]
                block.vector(v_tr)

            wstage = {"dve": 0, "qW": {0: 0, 1: 0}}

            kq_mm_last = {}
            wl_ctr = [0]

            def emit_kq_wload(wmat, h):
                wl = wl_ctr[0]
                wl_ctr[0] += 1
                p = wl % 2

                def s_w(g, h=h, p=p, wl=wl):
                    kb.wait(g, "ePOOL")
                    if wl >= 2 and (wl - 2) in kq_mm_last:
                        kb.wait(g, "ePE", kq_mm_last[wl - 2])
                    else:
                        kb.wait(g, "ePE")
                    srcw = wmat[:, h * 128:(h + 1) * 128].rearrange(
                        "(dc p) cc -> p dc cc", p=128)
                    dstw = wb[p].rearrange("p (dc cc) -> p dc cc", cc=128)
                    kb.inc(g.dma_start(out=dstw[:, 0:DC // 2, :],
                                       in_=srcw[:, 0:DC // 2, :]),
                           "qW%d" % (2 + p), 16)
                    kb.inc(g.dma_start(out=dstw[:, DC // 2:DC, :],
                                       in_=srcw[:, DC // 2:DC, :]),
                           "qW%d" % (2 + p), 16)
                block.gpsimd(s_w)
                return wl, p, ("qW%d" % (2 + p), c["qW%d" % (2 + p)])

            kq_ev = {}
            kq_st = {}

            def flush_evacs(pend):
                if not pend:
                    return

                def a_batch(a, bodies=tuple(pend)):
                    for body in bodies:
                        body(a)
                block.scalar(a_batch)
                pend.clear()

            def emit_kq_group(tag, dst, h, tc, n_tc, tok_off, scale, wload,
                              gate_tr, defer=None):
                wl, wp, WCV = wload
                gi = h * n_tc + tc
                par = gi % 2

                def t_mm(t_, h=h, tc=tc, gi=gi, par=par):
                    kb.wait(t_, WCV[0], WCV[1])
                    if gate_tr is not None:
                        kb.wait(t_, "eDVE", gate_tr)
                    if gi >= 2:
                        kb.wait(t_, "eACT", kq_ev[(tag, gi - 2)])
                    else:
                        kb.wait(t_, "eACT")
                    mm = None
                    for dc in range(DC):
                        mm = t_.matmul(
                            pb[par][:],
                            wb[wp][:, dc * 128:(dc + 1) * 128],
                            xlnT[:, dc * KV + tok_off + tc * 512:
                                 dc * KV + tok_off + (tc + 1) * 512],
                            start=(dc == 0), stop=(dc == DC - 1))
                    kb.inc(mm, "ePE")
                block.tensor(t_mm)
                MMC = c["ePE"]
                kq_mm_last[wl] = MMC

                def a_ev(a, h=h, tc=tc, gi=gi, par=par, MMC=MMC):
                    kb.wait(a, "ePE", MMC)
                    if gi >= 2:
                        kb.wait(a, "qA%d" % par, kq_st[(tag, gi - 2)])
                    else:
                        kb.wait(a, "qA%d" % par)
                    kb.inc(a.activation(ev[par][:], pb[par][:], AF.Copy,
                                        bias=0.0, scale=scale), "eACT")
                    kq_ev[(tag, gi)] = c["eACT"]
                    kb.wait(a, "eACT")
                    kb.inc(a.dma_start(
                        out=dst[h * 128:(h + 1) * 128, tc * 512:(tc + 1) * 512],
                        in_=ev[par][:]), "qA%d" % par, 16)
                    kq_st[(tag, gi)] = c["qA%d" % par]
                if defer is None:
                    block.scalar(a_ev)
                else:
                    defer.append(a_ev)

            VW = max(512, D // 4)
            NCH = D // VW
            NDH = VW // 512

            def emit_v_wload(ch, prev_piece_pe=None):
                vh = (ch % 2) * DC * VW

                def s_wv(g, ch=ch, vh=vh):
                    kb.wait(g, "ePOOL")
                    if prev_piece_pe is not None:
                        kb.wait(g, "ePE", prev_piece_pe)
                    else:
                        kb.wait(g, "ePE")
                    for rc in range(DC):
                        kb.inc(g.dma_start(
                            out=wvb[:, vh + rc * VW:vh + (rc + 1) * VW],
                            in_=wv[rc * 128:(rc + 1) * 128,
                                   ch * VW:(ch + 1) * VW]),
                            "qW%d" % (ch % 2), 16)
                block.gpsimd(s_wv)
                return ("qW%d" % (ch % 2), c["qW%d" % (ch % 2)])

            v_ev = {}
            v_st = {}
            vn_ctr = [0]

            def emit_v_tb(ch, tb, WV_CV, gate_tr, defer=None):
                for dhc in range(NDH):
                    n = vn_ctr[0]
                    vn_ctr[0] += 1
                    par = n % 2

                    def t_v(t_, ch=ch, tb=tb, dhc=dhc, n=n, par=par):
                        kb.wait(t_, WV_CV[0], WV_CV[1])
                        if gate_tr is not None:
                            kb.wait(t_, "eDVE", gate_tr)
                        if n >= 2:
                            kb.wait(t_, "eACT", v_ev[n - 2])
                        else:
                            kb.wait(t_, "eACT")
                        vh = (ch % 2) * DC * VW
                        mm = None
                        for dc in range(DC):
                            mm = t_.matmul(
                                pb[2 + par][:],
                                xlnT[:, dc * KV + tb * 128: dc * KV + (tb + 1) * 128],
                                wvb[:, vh + dc * VW + dhc * 512:
                                    vh + dc * VW + (dhc + 1) * 512],
                                start=(dc == 0), stop=(dc == DC - 1))
                        kb.inc(mm, "ePE")
                    block.tensor(t_v)
                    VMM = c["ePE"]

                    def a_ve(a, ch=ch, tb=tb, dhc=dhc, n=n, par=par, VMM=VMM):
                        kb.wait(a, "ePE", VMM)
                        if n >= 2:
                            kb.wait(a, "qA%d" % (2 + par), v_st[n - 2])
                        else:
                            kb.wait(a, "qA%d" % (2 + par))
                        kb.inc(a.activation(ev[2 + par][:], pb[2 + par][:], AF.Copy,
                                            bias=0.0, scale=1.0), "eACT")
                        v_ev[n] = c["eACT"]
                        kb.wait(a, "eACT")
                        kb.inc(a.dma_start(
                            out=v_s[tb * 128:(tb + 1) * 128,
                                    ch * VW + dhc * 512: ch * VW + (dhc + 1) * 512],
                            in_=ev[2 + par][:]), "qA%d" % (2 + par), 16)
                        v_st[n] = c["qA%d" % (2 + par)]
                    if defer is None:
                        block.scalar(a_ve)
                    else:
                        defer.append(a_ve)

            emit_block(0)
            emit_block(1)
            VCV0 = emit_v_wload(0)
            KW0 = emit_kq_wload(wk, 0)
            VCV1 = emit_v_wload(1) if NCH > 1 else None
            pend = []
            for t in range(2, NBKV):
                emit_block(t)
                tb = t - 2
                emit_v_tb(0, tb, VCV0, tr_cnt[tb], defer=pend)
                if NCH > 1:
                    emit_v_tb(1, tb, VCV1, tr_cnt[tb], defer=pend)
                flush_evacs(pend)
                if tb >= 3 and (tb - 3) % 4 == 0 and (tb - 3) // 4 < KV // 512:
                    tc = (tb - 3) // 4
                    emit_kq_group("k", kt_s, 0, tc, KV // 512, 0, 1.0, KW0,
                                  tr_cnt[4 * tc + 3])
            for tb in range(NBKV - 2, NBKV):
                emit_v_tb(0, tb, VCV0, tr_cnt[tb], defer=pend)
                if NCH > 1:
                    emit_v_tb(1, tb, VCV1, tr_cnt[tb], defer=pend)
                flush_evacs(pend)
            for tc in range((NBKV - 5) // 4 + 1, KV // 512):
                emit_kq_group("k", kt_s, 0, tc, KV // 512, 0, 1.0, KW0,
                              tr_cnt[min(NBKV - 1, 4 * tc + 3)])

            v_piece_pe = {0: c["ePE"], 1: c["ePE"]}
            for ch in range(2, NCH):
                VCV = emit_v_wload(ch, prev_piece_pe=v_piece_pe.get(ch - 2))
                for tb in range(NBKV):
                    emit_v_tb(ch, tb, VCV, None, defer=pend)
                    if len(pend) >= 2:
                        flush_evacs(pend)
                flush_evacs(pend)
                v_piece_pe[ch] = c["ePE"]

            for h in range(1, H):
                WL = emit_kq_wload(wk, h)
                for tc in range(KV // 512):
                    emit_kq_group("k", kt_s, h, tc, KV // 512, 0, 1.0, WL, None,
                                  defer=pend)
                    if len(pend) >= 2:
                        flush_evacs(pend)
                flush_evacs(pend)
            for h in range(H):
                WL = emit_kq_wload(wq, h)
                for tc in range(TOK // 512):
                    emit_kq_group("q", q_s, h, tc, TOK // 512, WNB * 128, 1.0,
                                  WL, None, defer=pend)
                    if len(pend) >= 2:
                        flush_evacs(pend)
                flush_evacs(pend)

        kb.barrier()
        if dev_stop == "p1":
            def s_f1(sync):
                kb.join(sync)
            block.sync(s_f1)
            return nc

        with contextlib.ExitStack() as _stk:
            qt_g = _stk.enter_context(nc.sbuf_tensor("p2_qt", [128, 2 * TOK], BF16))
            kt_g = _stk.enter_context(nc.sbuf_tensor("p2_kt", [128, 2 * KV], BF16))
            v_g = _stk.enter_context(nc.sbuf_tensor("p2_v", [128, 2 * KV], BF16))
            exp_g = _stk.enter_context(nc.sbuf_tensor("p2_exp", [128, 3 * TOK], BF16))
            esum_g = _stk.enter_context(nc.sbuf_tensor("p2_esum", [128, 4 * TOK], F32))
            esb_g = _stk.enter_context(nc.sbuf_tensor("p2_esb", [128, TOK], BF16))
            sqq_g = _stk.enter_context(nc.sbuf_tensor("p2_sqq", [128, KV], BF16))
            n2_g = _stk.enter_context(nc.sbuf_tensor("p2_n2", [1, KV + TOK + 16], F32))
            cbf_g = _stk.enter_context(nc.sbuf_tensor("p2_cbf", [1, 1], BF16))
            small_g = _stk.enter_context(nc.sbuf_tensor("p2_small", [128, 80], F32))
            sums_g = _stk.enter_context(nc.sbuf_tensor("p2_sums", [1, TOK], F32))
            rcpb_g = _stk.enter_context(nc.sbuf_tensor("p2_rcpb", [1, TOK], BF16))
            bc_g = _stk.enter_context(nc.sbuf_tensor("p2_bc", [128, 2 * 512], F32))
            rscr_g = _stk.enter_context(nc.sbuf_tensor("p2_rscr", [128, 1024], F32))
            ctxo_g = _stk.enter_context(nc.sbuf_tensor("p2_ctx", [128, TOK], BF16))

            qt = [qt_g.ap()[:, i * TOK:(i + 1) * TOK] for i in range(2)]
            kt = [kt_g.ap()[:, i * KV:(i + 1) * KV] for i in range(2)]
            vv = [v_g.ap()[:, i * KV:(i + 1) * KV] for i in range(2)]
            expT = [exp_g.ap()[:, i * TOK:(i + 1) * TOK] for i in range(3)]
            esum4 = [esum_g.ap()[:, i * TOK:(i + 1) * TOK] for i in range(4)]
            esb = esb_g.ap()
            sqq = sqq_g.ap()
            qn2 = n2_g.ap()[:, 0:TOK]
            kn2 = n2_g.ap()[:, TOK:TOK + KV]
            sml = n2_g.ap()[:, TOK + KV:TOK + KV + 16]
            cbf = cbf_g.ap()
            cb128 = small_g.ap()[:, 2 * (NBKV + 1):2 * (NBKV + 1) + 1]
            negcb = small_g.ap()[:, 2 * (NBKV + 1) + 2:2 * (NBKV + 1) + 3]
            rcpb = rcpb_g.ap()
            bc = [bc_g.ap()[:, i * 512:(i + 1) * 512] for i in range(2)]
            rscr = rscr_g.ap()
            ctxo = ctxo_g.ap()

            NJC = TOK // 512
            pev = {}
            rot = [0]

            def rot_wait(eng, n):
                if n >= 2 and (n - 2) in pev:
                    nm, val = pev[n - 2]
                    kb.wait(eng, nm, val)
                else:
                    kb.wait(eng, "eACT")
                    kb.wait(eng, "eDVE")

            hload = {}
            prok_pe = {}
            sqdone = {}
            bias_dve = {}
            sums_mm_pe = {}
            bcast_pe = {}
            cmulc = {}
            hdma = {}
            expj = {}
            avj = {}
            sumj = {}
            mskj = {}
            cbp_of = {}
            head_E0 = {}
            pcount = {}
            score_rot = {}
            rcp_dve = {}
            bias_pool = {}
            cmb_pool = {}

            def predict_head(h):
                E0 = c["eACT"]
                head_E0[h] = E0
                pcount[h] = [0]
                cum = 0
                for j in range(NBKV):
                    cum += len(mk_pieces(j)[2])
                    expj[(h, j)] = E0 + cum

            def emit_loads(h):
                def s_ld(sync, h=h):
                    p = h % 2
                    if h >= 2:
                        kb.wait(sync, "ePE", avj[(h - 2, NBKV - 1)])
                        kb.wait(sync, "ePOOL", sqdone[h - 2])
                    else:
                        kb.wait(sync, "ePOOL")
                    kb.inc(sync.dma_start(out=qt[p][:],
                                          in_=q_s[h * 128:(h + 1) * 128, :]),
                           "qH%d" % (h % 2), 16)
                    kb.inc(sync.dma_start(out=kt[p][:],
                                          in_=kt_s[h * 128:(h + 1) * 128, :]),
                           "qH%d" % (h % 2), 16)
                    srcv = v_s[:, h * 128:(h + 1) * 128].rearrange(
                        "(j p) cc -> p j cc", p=128)
                    dstv = vv[p].rearrange("p (j cc) -> p j cc", cc=128)
                    nj = NBKV // 4
                    for g in range(4):
                        kb.inc(sync.dma_start(out=dstv[:, g * nj:(g + 1) * nj, :],
                                              in_=srcv[:, g * nj:(g + 1) * nj, :]),
                               "qH%d" % (h % 2), 16)
                    hload[h] = c["qH%d" % (h % 2)]
                block.sync(s_ld)

            def emit_prologue_early(h):
                p = h % 2

                def g_sq(g, h=h, p=p):
                    kb.wait(g, "qH%d" % p, hload[h])
                    if h >= 1:
                        kb.wait(g, "ePE", prok_pe[h - 1])
                    kb.inc(g.tensor_tensor(sqq[:, 0:TOK], qt[p][:], qt[p][:],
                                           ALU.mult), "ePOOL")
                block.gpsimd(g_sq)
                SQQ = c["ePOOL"]

                nq, nk = TOK // 512, KV // 512
                for cc in range(nq):
                    n = rot[0]
                    rot[0] += 1

                    def t_q(t_, cc=cc, n=n, SQQ=SQQ):
                        kb.wait(t_, "ePOOL", SQQ)
                        rot_wait(t_, n)
                        kb.inc(t_.matmul(pw[n % 2][0:1, 0:512], ones_col[:],
                                         sqq[:, cc * 512:(cc + 1) * 512],
                                         start=True, stop=True), "ePE")
                    block.tensor(t_q)
                    QP = c["ePE"]

                    def v_q(v, cc=cc, n=n, QP=QP):
                        kb.wait(v, "ePE", QP)
                        kb.inc(v.tensor_reduce(sml[:, 4 + cc:5 + cc],
                                               pw[n % 2][0:1, 0:512],
                                               AX.X, ALU.max), "eDVE")
                        pev[n] = ("eDVE", c["eDVE"])
                    block.vector(v_q)
                QLAST = c["ePE"]

                def g_sqk(g, h=h, p=p, QLAST=QLAST):
                    kb.wait(g, "ePE", QLAST)
                    sqdone[h] = kb.inc(
                        g.tensor_tensor(sqq[:, 0:KV], kt[p][:], kt[p][:],
                                        ALU.mult), "ePOOL")
                block.gpsimd(g_sqk)
                SQK = c["ePOOL"]

                for cc in range(nk):
                    n = rot[0]
                    rot[0] += 1

                    def t_k(t_, cc=cc, n=n, SQK=SQK):
                        kb.wait(t_, "ePOOL", SQK)
                        rot_wait(t_, n)
                        kb.inc(t_.matmul(pw[n % 2][0:1, 0:512], ones_col[:],
                                         sqq[:, cc * 512:(cc + 1) * 512],
                                         start=True, stop=True), "ePE")
                    block.tensor(t_k)
                    KP = c["ePE"]

                    def v_k(v, cc=cc, n=n, KP=KP):
                        kb.wait(v, "ePE", KP)
                        kb.inc(v.tensor_reduce(sml[:, 8 + cc:9 + cc],
                                               pw[n % 2][0:1, 0:512],
                                               AX.X, ALU.max), "eDVE")
                        pev[n] = ("eDVE", c["eDVE"])
                    block.vector(v_k)
                prok_pe[h] = c["ePE"]

                def v_mx(v, h=h):
                    kb.wait(v, "eDVE")
                    v.tensor_reduce(sml[:, 0:1], sml[:, 4:8], AX.X, ALU.max)
                    kb.inc(v.tensor_reduce(sml[:, 1:2], sml[:, 8:16], AX.X,
                                           ALU.max), "eDVE")
                    kb.wait(v, "eDVE")
                    kb.inc(v.tensor_tensor(sml[:, 2:3], sml[:, 0:1], sml[:, 1:2],
                                           ALU.mult), "eDVE")
                block.vector(v_mx)

            def emit_prologue_late(h):
                p = h % 2

                def a_c(a, h=h):
                    kb.wait(a, "eDVE")
                    kb.inc(a.activation(sml[:, 3:4], sml[:, 2:3], AF.Sqrt,
                                        bias=0.0, scale=1.0), "eACT")
                block.scalar(a_c)

                def v_cbf(v, h=h):
                    kb.wait(v, "eACT")
                    if h >= 1:
                        kb.wait(v, "ePE", cbp_of[h - 1])
                    kb.inc(v.tensor_copy(cbf[:], sml[0:1, 3:4]), "eDVE")
                block.vector(v_cbf)
                CBF = c["eDVE"]

                n = rot[0]
                rot[0] += 1

                def t_cb(t_, h=h, n=n, CBF=CBF):
                    kb.wait(t_, "eDVE", CBF)
                    rot_wait(t_, n)
                    kb.inc(t_.matmul(pw[n % 2][:, 0:1], ones_row[:], cbf[:],
                                     start=True, stop=True), "ePE")
                block.tensor(t_cb)
                CBP = c["ePE"]
                cbp_of[h] = CBP

                def v_bias(v, h=h, p=p, n=n, CBP=CBP):
                    kb.wait(v, "ePE", CBP)
                    if h >= 1:
                        kb.wait(v, "eACT", expj[(h - 1, NBKV - 1)])
                    kb.inc(v.tensor_copy(cb128[:], pw[n % 2][:, 0:1]), "eDVE")
                    pev[n] = ("eDVE", c["eDVE"])
                    kb.wait(v, "eDVE")
                    kb.inc(v.tensor_scalar_mul(negcb[:], cb128[:], -QSCALE),
                           "eDVE")
                    bias_dve[h] = c["eDVE"]
                block.vector(v_bias)

                def g_bias(g, h=h, p=p):
                    kb.inc(g.memset(esum4[2 * p][:], 0.0), "ePOOL")
                    kb.inc(g.memset(esum4[2 * p + 1][:], 0.0), "ePOOL")
                    bias_pool[h] = c["ePOOL"]
                block.gpsimd(g_bias)

            def mk_pieces(j):
                w0 = max(0, j - WNB) * 128
                w1 = (min(NBQ - 1, j) + 1) * 128
                pieces = []
                a0 = w0
                while a0 < w1:
                    b0 = min(w1, a0 + 1024)
                    pieces.append((a0, b0))
                    a0 = b0
                return w0, w1, pieces

            piece_pe = {}

            def emit_sc(h, j):
                _, _, pieces = mk_pieces(j)
                rots = []
                for _pc in pieces:
                    n = rot[0]
                    rot[0] += 1
                    rots.append(n)
                    pcount[h][0] += 1
                    pev[n] = ("eACT", head_E0[h] + pcount[h][0])
                score_rot[(h, j)] = rots
                p = h % 2

                def t_sc(t_, h=h, j=j, pieces=pieces, rots=rots, p=p):
                    if j == 0:
                        kb.wait(t_, "eDVE", bias_dve[h])
                    for (a0, b0), n in zip(pieces, rots):
                        rot_wait(t_, n)
                        mm = t_.matmul(pw[n % 2][:, 0:min(512, b0 - a0)],
                                       kt[p][:, j * 128:(j + 1) * 128],
                                       qt[p][:, a0:min(a0 + 512, b0)],
                                       start=True, stop=True)
                        if b0 - a0 > 512:
                            mm = t_.matmul(pw[n % 2][:, 512:b0 - a0],
                                           kt[p][:, j * 128:(j + 1) * 128],
                                           qt[p][:, a0 + 512:b0],
                                           start=True, stop=True)
                        piece_pe[n] = kb.inc(mm, "ePE")
                block.tensor(t_sc)

            def emit_exp_head(h):
                def a_exp_all(a, h=h):
                    cnt = 0
                    for j in range(NBKV):
                        _, _, pieces = mk_pieces(j)
                        rots = score_rot[(h, j)]
                        for (a0, b0), n in zip(pieces, rots):
                            kb.wait(a, "ePE", piece_pe[n])
                            if j >= 3:
                                kb.wait(a, "ePE", avj[(h, j - 3)])
                                kb.wait(a, *sumj[(h, j - 3)])
                                if mskj.get((h, j - 3)) is not None:
                                    kb.wait(a, "eDVE", mskj[(h, j - 3)])
                            elif h >= 1:
                                jp = NBKV - 3 + j
                                kb.wait(a, "ePE", avj[(h - 1, jp)])
                                kb.wait(a, *sumj[(h - 1, jp)])
                                if mskj.get((h - 1, jp)) is not None:
                                    kb.wait(a, "eDVE", mskj[(h - 1, jp)])
                            got = kb.inc(
                                a.activation(expT[j % 3][:, a0:b0],
                                             pw[n % 2][:, 0:b0 - a0], AF.Exp,
                                             bias=negcb[:], scale=QSCALE),
                                "eACT")
                            cnt += 1
                            assert got == head_E0[h] + cnt, (h, j, got)
                        assert expj[(h, j)] == c["eACT"], (h, j)
                block.scalar(a_exp_all)

            def emit_msk(h, j):
                def g_msk(v, h=h, j=j):
                    did = False
                    if j <= NBQ - 1:
                        kb.wait(v, "eACT", expj[(h, j)])
                        qs0 = j * 128
                        kb.inc(v.tensor_tensor(expT[j % 3][:, qs0:qs0 + 128],
                                               expT[j % 3][:, qs0:qs0 + 128],
                                               m01_old[:], ALU.mult), "eDVE")
                        did = True
                    if j >= WNB:
                        kb.wait(v, "eACT", expj[(h, j)])
                        qs0 = (j - WNB) * 128
                        kb.inc(v.tensor_tensor(expT[j % 3][:, qs0:qs0 + 128],
                                               expT[j % 3][:, qs0:qs0 + 128],
                                               m01_diag[:], ALU.mult), "eDVE")
                        did = True
                    mskj[(h, j)] = c["eDVE"] if did else None
                block.vector(g_msk)

            SUM_GP = frozenset(list(range(0, 8)) + list(range(24, NBKV)))

            def emit_sum(h, j):
                w0, w1, _ = mk_pieces(j)
                gp = j in SUM_GP
                sem = "ePOOL" if gp else "eDVE"

                def x_sum(e, h=h, j=j, w0=w0, w1=w1, gp=gp, sem=sem):
                    kb.wait(e, "eACT", expj[(h, j)])
                    if mskj[(h, j)] is not None:
                        kb.wait(e, "eDVE", mskj[(h, j)])
                    if j in (0, WNB):
                        kb.wait(e, "ePOOL", bias_pool[h])
                    else:
                        kb.wait(e, *sumj[(h, j - 1)])
                    es = esum4[2 * (h % 2) + (0 if j < WNB else 1)]
                    sumj[(h, j)] = (sem, kb.inc(
                        e.tensor_tensor(es[:, w0:w1], es[:, w0:w1],
                                        expT[j % 3][:, w0:w1], ALU.add), sem))
                if gp:
                    block.gpsimd(x_sum)
                else:
                    block.vector(x_sum)

            def emit_av(h, j):
                w0, w1, _ = mk_pieces(j)
                p = h % 2

                def t_av(t_, h=h, j=j, w0=w0, w1=w1, p=p):
                    kb.wait(t_, "eACT", expj[(h, j)])
                    if mskj[(h, j)] is not None:
                        kb.wait(t_, "eDVE", mskj[(h, j)])
                    for ccx in range(w0 // 512, (w1 - 1) // 512 + 1):
                        a0 = max(w0, ccx * 512)
                        b0 = min(w1, (ccx + 1) * 512)
                        first = (j == 4 * ccx)
                        if first:
                            if h >= 1:
                                kb.wait(t_, "eDVE", cmulc[(h - 1, ccx)])
                            t_.matmul(pb[4 + ccx][:], ones128[:], zeros512[:],
                                      start=True, stop=False)
                        mm = t_.matmul(
                            pb[4 + ccx][:, a0 - ccx * 512: b0 - ccx * 512],
                            vv[p][:, j * 128:(j + 1) * 128],
                            expT[j % 3][:, a0:b0],
                            start=False,
                            stop=(j == min(NBKV - 1, 4 * ccx + 3 + WNB)))
                        kb.inc(mm, "ePE")
                    avj[(h, j)] = c["ePE"]
                block.tensor(t_av)

            def emit_fin_sums(h):
                p = h % 2

                def v_cmb(v, h=h, p=p):
                    if h >= 1:
                        kb.wait(v, "ePE", sums_mm_pe[h - 1])
                    kb.wait(v, *sumj[(h, WNB - 1)])
                    kb.wait(v, *sumj[(h, NBKV - 1)])
                    kb.inc(v.scalar_tensor_tensor(
                        out=esb[:], in0=esum4[2 * p][:], scalar=kvb_sb[:, 0:1],
                        in1=esum4[2 * p + 1][:], op0=ALU.mult, op1=ALU.add),
                        "eDVE")
                block.vector(v_cmb)
                CMB = c["eDVE"]

                for ccx in range(NJC):
                    n = rot[0]
                    rot[0] += 1

                    def t_sm(t_, ccx=ccx, n=n, CMB=CMB):
                        kb.wait(t_, "eDVE", CMB)
                        rot_wait(t_, n)
                        kb.inc(t_.matmul(pw[n % 2][0:1, 0:512], ones_col[:],
                                         esb[:, ccx * 512:(ccx + 1) * 512],
                                         start=True, stop=True), "ePE")
                    block.tensor(t_sm)
                    SMP = c["ePE"]

                    def v_sm(v, ccx=ccx, n=n, SMP=SMP, h=h):
                        kb.wait(v, "ePE", SMP)
                        if ccx == 0 and h >= 1:
                            kb.wait(v, "ePE", bcast_pe[(h - 1, NJC - 1)])
                        kb.inc(v.tensor_copy(rcpb[:, ccx * 512:(ccx + 1) * 512],
                                             pw[n % 2][0:1, 0:512]), "eDVE")
                        pev[n] = ("eDVE", c["eDVE"])
                        rcp_dve[(h, ccx)] = c["eDVE"]
                    block.vector(v_sm)
                sums_mm_pe[h] = c["ePE"]

            def emit_fin_chunk(h, ccx):
                n = rot[0]
                rot[0] += 1

                def t_b(t_, h=h, ccx=ccx, n=n):
                    kb.wait(t_, "eDVE", rcp_dve[(h, ccx)])
                    if ccx >= 2:
                        kb.wait(t_, "eDVE", cmulc[(h, ccx - 2)])
                    rot_wait(t_, n)
                    kb.inc(t_.matmul(pw[n % 2][:, 0:512], ones_row[:],
                                     rcpb[:, ccx * 512:(ccx + 1) * 512],
                                     start=True, stop=True), "ePE")
                block.tensor(t_b)
                BP = c["ePE"]
                bcast_pe[(h, ccx)] = BP

                def v_b(v, h=h, ccx=ccx, n=n, BP=BP):
                    kb.wait(v, "ePE", BP)
                    kb.inc(v.reciprocal(bc[ccx % 2][:], pw[n % 2][:, 0:512]),
                           "eDVE")
                    pev[n] = ("eDVE", c["eDVE"])
                    kb.wait(v, "eDVE")
                    kb.wait(v, "ePE", avj[(h, min(NBKV - 1, 4 * ccx + 3 + WNB))])
                    if ccx == 0 and h >= 1:
                        kb.wait(v, "qA%d" % ((h - 1) % 2), hdma[h - 1])
                    cmulc[(h, ccx)] = kb.inc(
                        v.tensor_tensor(ctxo[:, ccx * 512:(ccx + 1) * 512],
                                        pb[4 + ccx][:], bc[ccx % 2][:],
                                        ALU.mult), "eDVE")
                block.vector(v_b)

                if ccx == NJC - 1:
                    def g_st(g, h=h):
                        kb.wait(g, "eDVE", cmulc[(h, NJC - 1)])
                        for cx in range(NJC - 1):
                            kb.wait(g, "eDVE", cmulc[(h, cx)])
                        hdma[h] = kb.inc(
                            g.dma_start(out=ctx_s[h * 128:(h + 1) * 128, :],
                                        in_=ctxo[:]), "qA%d" % (h % 2), 16)
                    block.gpsimd(g_st)

            emit_loads(0)
            emit_prologue_early(0)
            emit_prologue_late(0)
            for h in range(H):
                predict_head(h)
                if h + 1 < H:
                    emit_loads(h + 1)
                emit_sc(h, 0)
                for j in range(NBKV):
                    if h >= 1:
                        if j == 0:
                            emit_fin_sums(h - 1)
                            emit_fin_chunk(h - 1, 0)
                        elif j in (4, 8, 12) and j // 4 < NJC:
                            emit_fin_chunk(h - 1, j // 4)
                    emit_msk(h, j)
                    emit_sum(h, j)
                    if j == NBKV // 2 and h + 1 < H:
                        emit_prologue_early(h + 1)
                    if j + 1 < NBKV:
                        emit_sc(h, j + 1)
                    emit_av(h, j)
                emit_exp_head(h)
                if h + 1 < H:
                    emit_prologue_late(h + 1)
            emit_fin_sums(H - 1)
            for ccx in range(NJC):
                emit_fin_chunk(H - 1, ccx)

        kb.barrier()
        if dev_stop == "p2":
            def s_f2(sync):
                kb.join(sync)
            block.sync(s_f2)
            return nc

        with contextlib.ExitStack() as _stk:
            wo_g = _stk.enter_context(nc.sbuf_tensor("wo_sb", [128, DC * D], BF16))
            wf3_g = _stk.enter_context(nc.sbuf_tensor("p3_wf", [128, 2 * D], F32))
            ctx3_g = _stk.enter_context(nc.sbuf_tensor("p3_ctx", [128, 2 * H * 512], BF16))
            ar_g = _stk.enter_context(nc.sbuf_tensor("p3_ar", [128, 2 * D], F32))
            x3_g = _stk.enter_context(nc.sbuf_tensor("p3_x", [128, 2 * D], F32))
            sq3_g = _stk.enter_context(nc.sbuf_tensor("p3_sq", [128, 2 * D], BF16))
            st3_g = _stk.enter_context(nc.sbuf_tensor("p3_st", [128, 16], F32))
            ln3_g = _stk.enter_context(nc.sbuf_tensor("p3_ln", [128, 2 * D], F32))
            xm3_g = _stk.enter_context(nc.sbuf_tensor("p3_xm", [128, 2 * D], BF16))
            wo_sb = wo_g.ap()
            wf3 = [wf3_g.ap()[:, i * D:(i + 1) * D] for i in range(2)]
            ctx3 = [ctx3_g.ap()[:, i * H * 512:(i + 1) * H * 512] for i in range(2)]
            ar = [ar_g.ap()[:, i * D:(i + 1) * D] for i in range(2)]
            x3 = [x3_g.ap()[:, i * D:(i + 1) * D] for i in range(2)]
            sq3 = [sq3_g.ap()[:, i * D:(i + 1) * D] for i in range(2)]
            st3 = [st3_g.ap()[:, i * 8:(i + 1) * 8] for i in range(2)]
            ln3 = [ln3_g.ap()[:, i * D:(i + 1) * D] for i in range(2)]
            xm3 = [xm3_g.ap()[:, i * D:(i + 1) * D] for i in range(2)]

            WO_D0 = c["eDVE"]
            for rc in range(DC):
                pw = rc % 2

                def s_wo(sync, rc=rc, pw=pw):
                    kb.wait(sync, "eDVE", WO_D0 + rc - 1 if rc >= 2 else WO_D0)
                    kb.inc(sync.dma_start(out=wf3[pw][:],
                                          in_=wo[rc * 128:(rc + 1) * 128, :]),
                           "qG%d" % (rc % 2), 16)
                block.sync(s_wo)
                WOLD = c["qG%d" % (rc % 2)]

                def v_wo(v, rc=rc, pw=pw):
                    kb.wait(v, "qG%d" % (rc % 2), WOLD)
                    kb.inc(v.tensor_copy(wo_sb[:, rc * D:(rc + 1) * D], wf3[pw][:]),
                           "eDVE")
                block.vector(v_wo)
            WO_DVE = c["eDVE"]

            ND4 = D // 512
            norm_act = {}
            res_dve = {}
            xmdma = {}
            ctxld = {}
            xld = {}
            AR3_P0 = c["ePE"]
            AR3_A0 = c["eACT"]
            def arev_cnt(n):
                tb2, d2 = divmod(n, ND4)
                return AR3_A0 + tb2 * (ND4 + 2) + d2 + 1
            armm = {}
            for qc in range(NQC):
                pq = qc % 2

                def s_ctx(sync, qc=qc, pq=pq):
                    if qc >= 2:
                        kb.wait(sync, "ePE", armm[(qc - 2, 3, ND4 - 1)])
                    else:
                        kb.wait(sync, "ePOOL")
                    for hh in range(H):
                        kb.inc(sync.dma_start(
                            out=ctx3[pq][:, hh * 512:(hh + 1) * 512],
                            in_=ctx_s[hh * 128:(hh + 1) * 128,
                                      qc * 512:(qc + 1) * 512]),
                            "qC%d" % (qc % 2), 16)
                    ctxld[qc] = c["qC%d" % (qc % 2)]
                block.sync(s_ctx)

                for m in range(4):
                    tb = qc * 4 + m
                    pm = tb % 2

                    def s_x(sync, tb=tb, pm=pm):
                        if tb >= 2:
                            kb.wait(sync, "eDVE", res_dve[tb - 2])
                        else:
                            kb.wait(sync, "ePOOL")
                        xld[tb] = kb.inc(sync.dma_start(
                            out=x3[pm][:],
                            in_=xkv[(WNB + tb) * 128:(WNB + tb + 1) * 128, :]),
                            "qX3%d" % (tb % 2), 16)
                    block.sync(s_x)

                    def t_ar(t_, qc=qc, m=m, pq=pq, tb=tb):
                        kb.wait(t_, "eDVE", WO_DVE)
                        kb.wait(t_, "qC%d" % (qc % 2), ctxld[qc])
                        for dc4 in range(ND4):
                            n = tb * ND4 + dc4
                            if n >= 2:
                                kb.wait(t_, "eACT", arev_cnt(n - 2))
                            mm = None
                            for hh in range(H):
                                mm = t_.matmul(
                                    pb[n % 2][:],
                                    ctx3[pq][:, hh * 512 + m * 128:
                                             hh * 512 + (m + 1) * 128],
                                    wo_sb[:, hh * D + dc4 * 512:
                                          hh * D + (dc4 + 1) * 512],
                                    start=(hh == 0), stop=(hh == H - 1))
                            armm[(qc, m, dc4)] = kb.inc(mm, "ePE")
                    block.tensor(t_ar)

                    def a_ar(a, qc=qc, m=m, tb=tb, pm=pm):
                        if tb >= 2:
                            kb.wait(a, "eDVE", norm_act[tb - 2])
                        for dc4 in range(ND4):
                            n = tb * ND4 + dc4
                            kb.wait(a, "ePE", armm[(qc, m, dc4)])
                            got = kb.inc(a.activation(
                                ar[pm][:, dc4 * 512:(dc4 + 1) * 512], pb[n % 2][:],
                                AF.Copy, bias=0.0, scale=1.0), "eACT")
                            assert got == arev_cnt(n), (got, arev_cnt(n))
                    block.scalar(a_ar)

                    pre = [("eACT", arev_cnt(tb * ND4 + ND4 - 1))]
                    if tb >= 2:
                        pre.append(("eDVE", norm_act[tb - 2]))
                        pre.append(("eDVE", res_dve[tb - 2]))
                    norm_act[tb] = _ln_chain(kb, block, ar[pm][:], ln3[pm][:],
                                             st3[pm], sq3[pm][:], eps_sb,
                                             pre_waits=pre)

                    def v_res(v, tb=tb, pm=pm):
                        kb.wait(v, "eDVE", norm_act[tb])
                        kb.wait(v, "qX3%d" % (tb % 2), xld[tb])
                        if tb >= 2:
                            kb.wait(v, "qA%d" % (tb % 2), xmdma[tb - 2])
                        res_dve[tb] = kb.inc(v.tensor_tensor(
                            xm3[pm][:], x3[pm][:], ln3[pm][:], ALU.add), "eDVE")
                    block.vector(v_res)

                    def a_xm(a, tb=tb, pm=pm):
                        kb.wait(a, "eDVE", res_dve[tb])
                        xmdma[tb] = kb.inc(a.dma_start(
                            out=xmid_s[tb * 128:(tb + 1) * 128, :], in_=xm3[pm][:]),
                            "qA%d" % (tb % 2), 16)
                    block.scalar(a_xm)

        kb.barrier()
        if dev_stop == "p3a":
            def s_f3(sync):
                kb.join(sync)
            block.sync(s_f3)
            return nc

        IHQ = IC // 4
        NGRP = max(1, IHQ // 4)
        GW = DC * 512
        DH8 = min(8, DC)
        NDG = DC // DH8
        with contextlib.ExitStack() as _stk:
            fxm_g = _stk.enter_context(nc.sbuf_tensor("f_xm", [128, 4 * D], BF16))
            fln2_g = _stk.enter_context(nc.sbuf_tensor("f_ln2", [128, 2 * D], BF16))
            ln2T_g = _stk.enter_context(nc.sbuf_tensor("f_ln2T", [128, DC * 512], BF16))
            fsq_g = _stk.enter_context(nc.sbuf_tensor("f_sq", [128, 2 * D], BF16))
            fst_g = _stk.enter_context(nc.sbuf_tensor("f_st", [128, 16], F32))
            fw8_g = _stk.enter_context(nc.sbuf_tensor("f_w8", [128, 2 * GW], I8))
            fwb_g = _stk.enter_context(nc.sbuf_tensor("f_wb", [128, 2 * DC * 256], BF16))
            fwd8_g = _stk.enter_context(nc.sbuf_tensor("f_wd8", [128, 4 * D], I8))
            fwdb_g = _stk.enter_context(nc.sbuf_tensor("f_wdb", [128, 4 * DH8 * 128], BF16))
            fgb_g = _stk.enter_context(nc.sbuf_tensor("f_gb", [128, 4 * 512], BF16))
            fuf_g = _stk.enter_context(nc.sbuf_tensor("f_uf", [128, 2 * 512], F32))
            fuse_g = _stk.enter_context(nc.sbuf_tensor("f_fuse", [128, IHQ * 512], BF16))
            fraw_g = _stk.enter_context(nc.sbuf_tensor("f_raw", [128, DC * 512], F32))
            fmb_g = _stk.enter_context(nc.sbuf_tensor("f_mb", [128, 2 * 512], BF16))
            fsqb_g = _stk.enter_context(nc.sbuf_tensor("f_sqb", [128, 2 * 512], BF16))
            flst_g = _stk.enter_context(nc.sbuf_tensor("f_lst", [1, 512 * 4], F32))
            flbf_g = _stk.enter_context(nc.sbuf_tensor("f_lbf", [1, 512 * 2], BF16))
            fbc_g = _stk.enter_context(nc.sbuf_tensor("f_bc", [128, 2 * 512], F32))
            flnc_g = _stk.enter_context(nc.sbuf_tensor("f_lnc", [128, DC * 512], BF16))
            flnT_g = _stk.enter_context(nc.sbuf_tensor("f_lnT", [128, 2 * D], BF16))
            fxo_g = _stk.enter_context(nc.sbuf_tensor("f_xo", [128, D], F32))

            fxm = fxm_g.ap()
            fln2 = [fln2_g.ap()[:, i * D:(i + 1) * D] for i in range(2)]
            ln2T = ln2T_g.ap()
            fsq = [fsq_g.ap()[:, i * D:(i + 1) * D] for i in range(2)]
            fst = [fst_g.ap()[:, i * 8:(i + 1) * 8] for i in range(2)]
            fw8 = [fw8_g.ap()[:, i * GW:(i + 1) * GW] for i in range(2)]
            fwb = [fwb_g.ap()[:, i * DC * 256:(i + 1) * DC * 256] for i in range(2)]
            fwd8 = [fwd8_g.ap()[:, i * 2 * D:(i + 1) * 2 * D] for i in range(2)]
            fwdb = [fwdb_g.ap()[:, i * 2 * DH8 * 128:(i + 1) * 2 * DH8 * 128] for i in range(2)]
            fgb = [fgb_g.ap()[:, i * 512:(i + 1) * 512] for i in range(4)]
            fuf = [fuf_g.ap()[:, i * 512:(i + 1) * 512] for i in range(2)]
            fuse = fuse_g.ap()
            fraw = fraw_g.ap()
            fmb = [fmb_g.ap()[:, i * 512:(i + 1) * 512] for i in range(2)]
            fsqb = [fsqb_g.ap()[:, i * 512:(i + 1) * 512] for i in range(2)]
            fl = [flst_g.ap()[:, i * 512:(i + 1) * 512] for i in range(4)]
            flbf = flbf_g.ap()
            fbc = [fbc_g.ap()[:, i * 512:(i + 1) * 512] for i in range(2)]
            flnc = flnc_g.ap()
            flnT = [flnT_g.ap()[:, i * D:(i + 1) * D] for i in range(2)]
            fxo = [fxo_g.ap(), fxo_g.ap()]

            wg_r = wg_t.rearrange("p (dc i) -> p dc i", i=I)
            wu_r = wu_t.rearrange("p (dc i) -> p dc i", i=I)

            ydma_m = {}
            ydma_tc = {}
            for tc in range(NQC):
                def s_xm(sync, tc=tc):
                    if tc >= 1:
                        kb.wait(sync, "qA0", ydma_tc[tc - 1][0])
                        kb.wait(sync, "qA1", ydma_tc[tc - 1][1])
                    else:
                        kb.wait(sync, "ePOOL")
                    for m in range(4):
                        tb = tc * 4 + m
                        kb.inc(sync.dma_start(out=fxm[:, m * D:(m + 1) * D],
                                              in_=xmid_s[tb * 128:(tb + 1) * 128, :]),
                               "qM", 16)
                block.sync(s_xm)
                XM_LD = c["qM"]

                tr2n = {}
                tr2t = {}
                tr2pe = {}
                for m in range(4):
                    pm = m % 2

                    pre = [("qM", XM_LD)]
                    if m >= 2:
                        pre.append(("eDVE", tr2n[m - 2]))
                    if tc >= 1 and m < 2:
                        pre.append(("eDVE", c["eDVE"]))
                    extra = [("ePE", tr2pe[m - 2])] if m >= 2 else None
                    tr2n[m] = _ln_chain(kb, block, fxm[:, m * D:(m + 1) * D],
                                        fln2[pm][:], fst[pm], fsq[pm][:],
                                        eps_sb, pre_waits=pre, extra_norm_waits=extra)

                    def t_tr2(t_, tc=tc, m=m, pm=pm):
                        kb.wait(t_, "eDVE", tr2n[m])
                        if m == 0 and tc >= 1:
                            kb.wait(t_, "ePE", s_tr2_prev[0])
                        if m >= 2:
                            kb.wait(t_, "eDVE", tr2t[m - 2])
                        else:
                            kb.wait(t_, "eDVE")
                        for dc in range(DC):
                            bank, k = divmod(dc, 4)
                            mm = t_.matmul(pn[bank][:, k * 128:(k + 1) * 128],
                                           fln2[pm][:, dc * 128:(dc + 1) * 128],
                                           id128[:], start=(k == 0), stop=(k == 3))
                            if k == 3 or dc == DC - 1:
                                kb.inc(mm, "ePE")
                    block.tensor(t_tr2)
                    TR2P = c["ePE"]
                    tr2pe[m] = TR2P

                    def v_tr2(v, tc=tc, m=m, TR2P=TR2P):
                        kb.wait(v, "ePE", TR2P)
                        xr2 = ln2T.rearrange("p (dc w) -> p dc w", w=512)
                        for bank in range((DC + 3) // 4):
                            nb = min(4, DC - bank * 4)
                            dst = xr2[:, bank * 4:bank * 4 + nb,
                                      m * 128:(m + 1) * 128]
                            srcw = pn[bank][:, 0:nb * 128].rearrange(
                                "p (k cc) -> p k cc", cc=128)
                            kb.inc(v.tensor_copy(dst, srcw), "eDVE")
                        tr2t[m] = c["eDVE"]
                    block.vector(v_tr2)
                LN2T_LD = c["eDVE"]

                DOWN_DVE_PREV = [c["eDVE"]]
                mm_of_step = {}
                ev_of_step = {}
                ld_used = {}
                step_ctr = [0]
                for iq in range(4):
                    wld = {}
                    gmm = {}
                    gact = {}
                    gdve = {}
                    gpool = {}
                    for grp in range(NGRP):
                        for phase in range(2):
                            li = (iq * NGRP + grp) * 2 + phase
                            pwb = li % 2

                            def s_w2(sync, tc=tc, iq=iq, grp=grp, phase=phase,
                                     li=li, pwb=pwb):
                                if li >= 2 and (li - 2) in ld_used:
                                    kb.wait(sync, "eDVE", ld_used[li - 2])
                                else:
                                    kb.wait(sync, "eDVE")
                                wr = wu_r if phase else wg_r
                                g0 = iq * NGRP + grp
                                srcw = wr[:, :, g0 * 512:(g0 + 1) * 512]
                                dstw = fw8[pwb].rearrange("p (dc i) -> p dc i", i=512)
                                kb.inc(sync.dma_start(out=dstw[:, 0:DC // 2, :],
                                                      in_=srcw[:, 0:DC // 2, :]),
                                       "qG%d" % (li % 3), 16)
                                kb.inc(sync.dma_start(out=dstw[:, DC // 2:DC, :],
                                                      in_=srcw[:, DC // 2:DC, :]),
                                       "qG%d" % (li % 3), 16)
                                wld[li] = c["qG%d" % (li % 3)]
                            block.sync(s_w2)

                            silu_parts = []
                            for gpair in range(2):
                                il0 = grp * 4 + gpair * 2
                                stp = step_ctr[0]
                                step_ctr[0] += 1
                                pi = stp % 2
                                B = [pb[4 * (stp % 2)], pb[4 * (stp % 2) + 1]]

                                def v_cv(v, li=li, gpair=gpair, pi=pi, pwb=pwb,
                                         stp=stp):
                                    kb.wait(v, "qG%d" % (li % 3), wld[li])
                                    if stp >= 2:
                                        kb.wait(v, "ePE", mm_of_step[stp - 2])
                                    else:
                                        kb.wait(v, "ePE")
                                    srcw = fw8[pwb].rearrange(
                                        "p (dc i) -> p dc i", i=512)[
                                        :, :, gpair * 256:(gpair + 1) * 256]
                                    dstw = fwb[pi].rearrange(
                                        "p (dc i) -> p dc i", i=256)
                                    kb.inc(v.tensor_copy(dstw[:, :, :],
                                                         srcw[:, :, :]), "eDVE")
                                    ld_used[li] = c["eDVE"]
                                block.vector(v_cv)
                                CV = c["eDVE"]

                                def t_m(t_, tc=tc, iq=iq, phase=phase, il0=il0,
                                        pi=pi, CV=CV, stp=stp, B=B):
                                    kb.wait(t_, "eDVE", CV)
                                    if iq == 0 and stp % 256 == 0 and phase == 0:
                                        kb.wait(t_, "eDVE", LN2T_LD)
                                    if iq > 0 and il0 == 0 and phase == 0:
                                        kb.wait(t_, "eDVE", DOWN_DVE_PREV[0])
                                    if stp >= 2 and ev_of_step.get(stp - 2):
                                        nm, val = ev_of_step[stp - 2]
                                        kb.wait(t_, nm, val)
                                    else:
                                        kb.wait(t_, "eACT")
                                        kb.wait(t_, "eDVE")
                                    for sub in range(2):
                                        mm = None
                                        for dc in range(DC):
                                            mm = t_.matmul(
                                                B[sub][:],
                                                fwb[pi][:, dc * 256 + sub * 128:
                                                        dc * 256 + (sub + 1) * 128],
                                                ln2T[:, dc * 512:(dc + 1) * 512],
                                                start=(dc == 0), stop=(dc == DC - 1))
                                        gmm[(phase, il0 + sub)] = kb.inc(mm, "ePE")
                                    mm_of_step[stp] = c["ePE"]
                                block.tensor(t_m)

                                if phase == 0:
                                    silu_parts.append((il0, stp, B))
                                else:
                                    def v_u(v, il0=il0, iq=iq, stp=stp, B=B):
                                        for sub in range(2):
                                            il = il0 + sub
                                            icb = iq * IHQ + il
                                            kb.wait(v, "ePE", gmm[(1, il)])
                                            if il >= 2:
                                                kb.wait(v, "ePOOL", gpool[il - 2])
                                            else:
                                                kb.wait(v, "ePOOL")
                                            gdve[il] = kb.inc(v.tensor_scalar(
                                                fuf[il % 2][:], B[sub][:],
                                                su_sb[:, icb:icb + 1], None,
                                                ALU.mult), "eDVE")
                                        ev_of_step[stp] = ("eDVE", c["eDVE"])
                                    block.vector(v_u)

                                    def g_f(g, tc=tc, iq=iq, il0=il0):
                                        for sub in range(2):
                                            il = il0 + sub
                                            kb.wait(g, "eACT", gact[il])
                                            kb.wait(g, "eDVE", gdve[il])
                                            if il < 2 and (iq > 0 or tc > 0):
                                                kb.wait(g, "ePE")
                                            gpool[il] = kb.inc(g.tensor_tensor(
                                                fuse[:, il * 512:(il + 1) * 512],
                                                fgb[il % 4][:], fuf[il % 2][:],
                                                ALU.mult), "ePOOL")
                                    block.gpsimd(g_f)
                            if phase == 0:
                                def a_g_all(a, iq=iq, parts=tuple(silu_parts)):
                                    for il0, stp, B in parts:
                                        for sub in range(2):
                                            il = il0 + sub
                                            icb = iq * IHQ + il
                                            kb.wait(a, "ePE", gmm[(0, il)])
                                            if il >= 4:
                                                kb.wait(a, "ePOOL", gpool[il - 4])
                                            else:
                                                kb.wait(a, "ePOOL")
                                            gact[il] = kb.inc(a.activation(
                                                fgb[il % 4][:], B[sub][:], AF.Silu,
                                                bias=0.0,
                                                scale=sg_sb[:, icb:icb + 1]),
                                                "eACT")
                                        ev_of_step[stp] = ("eACT", c["eACT"])
                                block.scalar(a_g_all)
                    FUSE_POOL = c["ePOOL"]
                    GU_PE_LAST = c["ePE"]

                    for dg in range(NDG):
                        dmm = {}
                        dcv = {}
                        wdl = {}
                        WD_A0 = c["eACT"]
                        WD_PE0 = c["ePE"]
                        for pr in range(IHQ // 2):
                            dcv[pr] = WD_A0 + pr + 1
                        for pr in range(IHQ // 2):
                            il0 = 2 * pr
                            icb0 = iq * IHQ + il0
                            pp = pr % 2

                            def s_wd(sync, icb0=icb0, pr=pr, pp=pp):
                                if pr >= 2:
                                    kb.wait(sync, "eACT", dcv[pr - 2])
                                else:
                                    kb.wait(sync, "eACT", WD_A0)
                                kb.inc(sync.dma_start(
                                    out=fwd8[pp][:],
                                    in_=wd_t[:, icb0 * D:(icb0 + 2) * D]),
                                    "qD%d" % pp, 16)
                                wdl[pr] = c["qD%d" % pp]
                            block.sync(s_wd)

                            def t_d(t_, pr=pr, pp=pp, iq=iq, dg=dg, il0=il0):
                                kb.wait(t_, "eACT", dcv[pr])
                                if pr == 0:
                                    kb.wait(t_, "ePOOL", FUSE_POOL)
                                    kb.wait(t_, "eDVE")
                                for sub in range(2):
                                    il = 2 * pr + sub
                                    mm = None
                                    for d8 in range(DH8):
                                        mm = t_.matmul(
                                            pb[d8][:],
                                            fwdb[pp][:, sub * DH8 * 128 + d8 * 128:
                                                    sub * DH8 * 128 + (d8 + 1) * 128],
                                            fuse[:, il * 512:(il + 1) * 512],
                                            start=(pr == 0 and sub == 0),
                                            stop=(pr == IHQ // 2 - 1 and sub == 1))
                                dmm[pr] = kb.inc(mm, "ePE")
                            block.tensor(t_d)

                        def a_wd_all(a, dg=dg, WD_A0=WD_A0, WD_PE0=WD_PE0):
                            for pr in range(IHQ // 2):
                                pp = pr % 2
                                kb.wait(a, "qD%d" % pp, wdl[pr])
                                if pr >= 2:
                                    kb.wait(a, "ePE", dmm[pr - 2])
                                else:
                                    kb.wait(a, "ePE", WD_PE0)
                                srcw = fwd8[pp].rearrange(
                                    "p (s d) -> p s d", d=D)[
                                    :, :, dg * DH8 * 128:(dg + 1) * DH8 * 128]
                                dstw = fwdb[pp].rearrange(
                                    "p (s w) -> p s w", w=DH8 * 128)
                                got = kb.inc(a.activation(
                                    dstw[:, :, :], srcw[:, :, :], AF.Copy,
                                    bias=0.0, scale=1.0), "eACT")
                                assert got == dcv[pr], (dg, pr, got)
                        block.scalar(a_wd_all)
                        DMM = c["ePE"]

                        def v_dev(v, dg=dg, iq=iq, DMM=DMM):
                            kb.wait(v, "ePE", DMM)
                            if iq >= 1:
                                kb.wait(v, "eDVE", DOWN_DVE_PREV[0])
                            for d8 in range(DH8):
                                dout_c = dg * DH8 + d8
                                dst = fraw[:, dout_c * 512:(dout_c + 1) * 512]
                                if iq == 0:
                                    ins = v.tensor_scalar(
                                        dst, pb[d8][:], sd_sb[:, dout_c:dout_c + 1],
                                        None, ALU.mult)
                                else:
                                    ins = v.scalar_tensor_tensor(
                                        out=dst, in0=pb[d8][:],
                                        scalar=sd_sb[:, dout_c:dout_c + 1], in1=dst,
                                        op0=ALU.mult, op1=ALU.add)
                                kb.inc(ins, "eDVE")
                        block.vector(v_dev)
                    DOWN_DVE_PREV[0] = c["eDVE"]
                s_tr2_prev = [GU_PE_LAST]
                RAW_DVE = c["eDVE"]

                MB_P0 = c["ePE"]
                mba = {}
                sq2a = {}

                def a_stats(a, tc=tc, MB_P0=MB_P0):
                    kb.wait(a, "eDVE", RAW_DVE)
                    kb.wait(a, "ePE", MB_P0)
                    for dc in range(DC):
                        if dc >= 2:
                            kb.wait(a, "ePE", MB_P0 + dc - 1)
                        mba[dc] = kb.inc(a.activation(
                            fmb[dc % 2][:], fraw[:, dc * 512:(dc + 1) * 512],
                            AF.Copy, bias=0.0, scale=1.0), "eACT")
                    for dc in range(DC):
                        if dc >= 2:
                            kb.wait(a, "ePE", MB_P0 + DC + dc - 1)
                        sq2a[dc] = kb.inc(a.activation(
                            fsqb[dc % 2][:], fraw[:, dc * 512:(dc + 1) * 512],
                            AF.Square, bias=0.0, scale=1.0), "eACT")
                block.scalar(a_stats)

                def t_stats(t_, tc=tc, MB_P0=MB_P0):
                    for dc in range(DC):
                        kb.wait(t_, "eACT", mba[dc])
                        got = kb.inc(t_.matmul(
                            pb[0][0:1, :], ones_col[:], fmb[dc % 2][:],
                            start=(dc == 0), stop=(dc == DC - 1)), "ePE")
                        assert got == MB_P0 + dc + 1
                    for dc in range(DC):
                        kb.wait(t_, "eACT", sq2a[dc])
                        got = kb.inc(t_.matmul(
                            pb[1][0:1, :], ones_col[:], fsqb[dc % 2][:],
                            start=(dc == 0), stop=(dc == DC - 1)), "ePE")
                        assert got == MB_P0 + DC + dc + 1
                block.tensor(t_stats)
                SQ2_PE = c["ePE"]

                def v_ls(v, tc=tc):
                    kb.wait(v, "ePE", SQ2_PE)
                    v.tensor_scalar_mul(fl[0][:], pb[0][0:1, :], 1.0 / D)
                    kb.inc(v.tensor_scalar_mul(fl[1][:], pb[1][0:1, :], 1.0 / D),
                           "eDVE")
                    kb.wait(v, "eDVE")
                    kb.inc(v.scalar_tensor_tensor(
                        out=fl[2][:], in0=fl[0][:], scalar=1.0, in1=fl[0][:],
                        op0=ALU.mult, op1=ALU.mult), "eDVE")
                    kb.wait(v, "eDVE")
                    kb.inc(v.tensor_tensor(fl[1][:], fl[2][:], fl[1][:],
                                           ALU.subtract), "eDVE")
                    kb.wait(v, "eDVE")
                    kb.inc(v.tensor_scalar_mul(fl[1][:], fl[1][:], -1.0), "eDVE")
                block.vector(v_ls)

                def a_lsd(a, tc=tc):
                    kb.wait(a, "eDVE")
                    kb.inc(a.activation(fl[3][:], fl[1][:], AF.Sqrt,
                                        bias=eps_sb[0:1, :], scale=1.0), "eACT")
                block.scalar(a_lsd)

                def v_lr(v, tc=tc):
                    kb.wait(v, "eACT")
                    kb.inc(v.reciprocal(fl[1][:], fl[3][:]), "eDVE")
                    kb.wait(v, "eDVE")
                    kb.inc(v.scalar_tensor_tensor(
                        out=fl[2][:], in0=fl[0][:], scalar=-1.0, in1=fl[1][:],
                        op0=ALU.mult, op1=ALU.mult), "eDVE")
                    kb.wait(v, "eDVE")
                    kb.inc(v.tensor_copy(flbf[:, 0:512], fl[1][:]), "eDVE")
                    kb.inc(v.tensor_copy(flbf[:, 512:1024], fl[2][:]), "eDVE")
                block.vector(v_lr)
                LSTD = c["eDVE"]

                def t_bc2(t_, tc=tc):
                    kb.wait(t_, "eDVE", LSTD)
                    kb.inc(t_.matmul(pb[2][:], ones_row[:], flbf[:, 0:512],
                                     start=True, stop=True), "ePE")
                    kb.inc(t_.matmul(pb[3][:], ones_row[:], flbf[:, 512:1024],
                                     start=True, stop=True), "ePE")
                block.tensor(t_bc2)
                BC2 = c["ePE"]

                def v_bc2(v, tc=tc):
                    kb.wait(v, "ePE", BC2)
                    kb.inc(v.tensor_copy(fbc[0][:], pb[2][:]), "eDVE")
                    kb.inc(v.tensor_copy(fbc[1][:], pb[3][:]), "eDVE")
                block.vector(v_bc2)
                FBC = c["eDVE"]

                lncp = {}
                for dc in range(DC):
                    def v_ln1(v, tc=tc, dc=dc):
                        kb.wait(v, "eDVE", FBC)
                        kb.wait(v, "ePE", SQ2_PE)
                        kb.inc(v.tensor_tensor(
                            fraw[:, dc * 512:(dc + 1) * 512],
                            fraw[:, dc * 512:(dc + 1) * 512], fbc[0][:], ALU.mult),
                            "eDVE")
                    block.vector(v_ln1)
                    VM1 = c["eDVE"]

                    def g_ln2(g, tc=tc, dc=dc):
                        kb.wait(g, "eDVE", VM1)
                        if dc == 0:
                            kb.wait(g, "qT30")
                            kb.wait(g, "qT31")
                        lncp[dc] = kb.inc(g.tensor_tensor(
                            flnc[:, dc * 512:(dc + 1) * 512],
                            fraw[:, dc * 512:(dc + 1) * 512], fbc[1][:], ALU.add),
                            "ePOOL")
                    block.gpsimd(g_ln2)

                trd = {}
                for m in range(4):
                    pm = m % 2

                    def t_tr3(t_, tc=tc, m=m, pm=pm):
                        for dc in range(DC):
                            kb.wait(t_, "ePOOL", lncp[dc])
                        if m >= 1:
                            kb.wait(t_, "eDVE", trd[m - 1])
                        else:
                            kb.wait(t_, "eDVE")
                        for dc in range(DC):
                            bank, k = divmod(dc, 4)
                            mm = t_.matmul(pn[bank][:, k * 128:(k + 1) * 128],
                                           flnc[:, dc * 512 + m * 128:
                                                dc * 512 + (m + 1) * 128],
                                           id128[:], start=(k == 0), stop=(k == 3))
                            if k == 3 or dc == DC - 1:
                                kb.inc(mm, "ePE")
                    block.tensor(t_tr3)
                    TR3P = c["ePE"]

                    def v_tr3(v, tc=tc, m=m, pm=pm, TR3P=TR3P):
                        kb.wait(v, "ePE", TR3P)
                        for bank in range((DC + 3) // 4):
                            nb = min(4, DC - bank * 4)
                            dst = flnT[pm][:, bank * 4 * 128:(bank * 4 + nb) * 128]
                            srcw = pn[bank][:, 0:nb * 128]
                            kb.inc(v.tensor_copy(dst, srcw), "eDVE")
                    block.vector(v_tr3)
                    TR3 = c["eDVE"]

                    def v_xo(v, tc=tc, m=m, pm=pm, TR3=TR3):
                        kb.wait(v, "eDVE", TR3)
                        if m >= 1:
                            kb.wait(v, "qA%d" % ((m - 1) % 2), ydma_m[(tc, m - 1)])
                        elif tc >= 1:
                            kb.wait(v, "qA1", ydma_tc[tc - 1][1])
                        trd[m] = kb.inc(v.tensor_tensor(
                            fxo[pm][:], fxm[:, m * D:(m + 1) * D], flnT[pm][:],
                            ALU.add), "eDVE")
                    block.vector(v_xo)
                    XO = c["eDVE"]

                    def a_y(a, tc=tc, m=m, pm=pm, XO=XO):
                        kb.wait(a, "eDVE", XO)
                        tb = tc * 4 + m
                        ydma_m[(tc, m)] = kb.inc(a.dma_start(
                            out=y[tb * 128:(tb + 1) * 128, :], in_=fxo[pm][:]),
                            "qA%d" % (m % 2), 16)
                    block.scalar(a_y)
                ydma_tc[tc] = (c["qA0"], c["qA1"])


        def s_fin(sync):
            kb.join(sync)
        block.sync(s_fin)

    return nc


def _prep_core_inputs(inputs, core, cfg=CFG):
    D, Iw = cfg.D, cfg.I
    b, half = core // 2, core % 2
    x = np.asarray(inputs["L0_attn_x"])
    TOK, WNB, NBKV = cfg.TOK, cfg.WNB, cfg.NBKV
    q0 = half * TOK
    lo = q0 - WNB * 128
    if lo < 0:
        pad = np.zeros((-lo, D), dtype=np.float32)
        xkv = np.concatenate([pad, np.asarray(x[b, 0:q0 + TOK], dtype=np.float32)],
                             axis=0)
        npad = -lo
    else:
        xkv = np.ascontiguousarray(np.asarray(x[b, lo:q0 + TOK], dtype=np.float32))
        npad = 0
    kvb = np.full((128, NBKV), 0.0 if npad else 1.0, dtype=np.float32)
    DC, IC = cfg.DC, cfg.IC
    wg = np.asarray(inputs["L0_ffn_w_gate"])
    wu = np.asarray(inputs["L0_ffn_w_up"])
    wd = np.asarray(inputs["L0_ffn_w_down"])
    wg_t = np.ascontiguousarray(
        wg.reshape(DC, 128, Iw).transpose(1, 0, 2)).reshape(128, DC * Iw)
    wu_t = np.ascontiguousarray(
        wu.reshape(DC, 128, Iw).transpose(1, 0, 2)).reshape(128, DC * Iw)
    wd_t = np.ascontiguousarray(
        wd.reshape(IC, 128, D).transpose(1, 0, 2)).reshape(128, IC * D)
    sg = np.ascontiguousarray(np.asarray(inputs["L0_ffn_s_gate"],
                                         dtype=np.float32).reshape(IC, 128).T)
    su = np.ascontiguousarray(np.asarray(inputs["L0_ffn_s_up"],
                                         dtype=np.float32).reshape(IC, 128).T)
    sd = np.ascontiguousarray(np.asarray(inputs["L0_ffn_s_down"],
                                         dtype=np.float32).reshape(DC, 128).T)
    return dict(
        xkv=np.ascontiguousarray(xkv, dtype=np.float32),
        wq=np.ascontiguousarray(np.asarray(inputs["L0_attn_Wq"], dtype=np.float32)),
        wk=np.ascontiguousarray(np.asarray(inputs["L0_attn_Wk"], dtype=np.float32)),
        wv=np.ascontiguousarray(np.asarray(inputs["L0_attn_Wv"], dtype=np.float32)),
        wo=np.ascontiguousarray(np.asarray(inputs["L0_attn_Wo"], dtype=np.float32)),
        wg_t=wg_t, wu_t=wu_t, wd_t=wd_t, sg=sg, su=su, sd=sd, kvb=kvb,
    )


def kernel(**inputs):
    cfg = CFG
    nc = build_program(cfg)
    in_maps = [_prep_core_inputs(inputs, core, cfg) for core in range(8)]
    res = run_bass_kernel_spmd(nc, in_maps, core_ids=list(range(8)))
    x = np.asarray(inputs["L0_attn_x"])
    B, S, D = x.shape
    out = np.empty((B, S, D), dtype=np.float32)
    for core in range(8):
        b, half = core // 2, core % 2
        out[b, half * cfg.TOK:(half + 1) * cfg.TOK] = res.results[core]["y"]
    return out
